# version 02
# baseline (speedup 1.0000x reference)
import os
import numpy as np
import ml_dtypes
import concourse.bass as bass
import concourse.mybir as mybir

F32 = mybir.dt.float32
BF16 = mybir.dt.bfloat16
U8 = mybir.dt.uint8
AF = mybir.ActivationFunctionType
ALU = mybir.AluOpType
AX = mybir.AxisListType
BF = ml_dtypes.bfloat16

B, S, H, NH, DH, ROT = 2, 4096, 2048, 16, 128, 32
NE, INTER = 32, 768
TOK = 1024
NT = 8
KC = 16
LN_EPS = 1e-5
KB = 1024

A0, B0, C0, D0, E0, F0 = 0, 64 * KB, 96 * KB, 128 * KB, 160 * KB, 202 * KB
ARENA = 205 * KB


def stripe_pos(s):
    return (s, 0) if s < 4 else (7 - s, 1)


class Sem:
    def __init__(self, nc, name):
        self.h = nc.semaphore(name).__enter__()
        self.n = 0

    def inc(self, inst, k=1):
        inst.then_inc(self.h, k)
        self.n += k

    def wait(self, eng, val=None):
        eng.wait_ge(self.h, self.n if val is None else val)


class MK:
    def __init__(self, taps=(), stage=9):
        self.taps = set(taps)
        self.stage = stage
        nc = bass.Bass("TRN2", target_bir_lowering=False, debug=False,
                       detect_race_conditions=False)
        self.nc = nc
        self.t = {}
        self.sem = {}
        self._declare_dram()
        self._alloc_sbuf()
        self._alloc_psum()
        self.blk = nc.Block()
        self.blk.__enter__()
        for name in ("ldc ldm ldg ldx tr1 xh1 ln1 sq1 sq6 wqkL0 wqkL1 wvL0 wvL1 qkps qkev sqps sqev "
                     "rope ccw0 ccw1 ccwV0 ccwV1 vtps vtev ag qdone mks kvL0 kvL1 qk5 rx5 ex5 f5 fx5 "
                     "sx5 pt5 ptev pv5 rb5 rbe oev df5 rv5 woL0 woL1 ldx60 ldx61 aops aoev tr6 x2w "
                     "ln2 ln2a ln2b xh2s xh2 x2d gps gev rt7 cmb wguL0 wguL1 wdL0 wdL1 ceps ceev "
                     "gups guevS guevV hrd dps dev tr9 ytv x2r yst r7a r7b tap").split():
            self.sem[name] = Sem(nc, name)
        self._emit()
        self.blk.__exit__(None, None, None)

    def _declare_dram(self):
        nc = self.nc

        def di(name, shape, dtype):
            self.t[name] = nc.dram_tensor(name, shape, dtype, kind="ExternalInput").ap()

        di("xin", [TOK, H], F32)
        di("costab", [ROT, TOK], BF16)
        di("sintab", [ROT, TOK], BF16)
        di("dtab", [128, 48], F32)
        di("kidx", [128, 512], F32)
        di("ident_bf", [128, 128], BF16)
        di("ident_f", [128, 128], F32)
        di("smat", [128, ROT], BF16)
        di("selmat", [NE, NE * 128], BF16)
        di("ones_bf", [1, 128], BF16)
        di("wqk", [32, 128, KC * 128], F32)
        di("bqk", [128, 32], F32)
        di("wv", [8, 128, KC * 256], F32)
        di("bv", [1, H], F32)
        di("wout", [16, 128, KC * 128], F32)
        di("bout", [128, 16], F32)
        di("gate", [128, KC * NE], F32)
        di("wgu", [NE, 4, 128, 6144], F32)
        di("wd", [NE // 2, 4, 128, 6144], F32)
        self.t["y"] = nc.dram_tensor("y", [TOK, H], F32, kind="ExternalOutput").ap()
        self.t["cc_in"] = nc.dram_tensor("cc_in", [2 * NH, 128 * TOK], BF16,
                                         kind="Internal").ap()
        self.t["cc_out"] = nc.dram_tensor("cc_out", [8 * NH, 128 * TOK], BF16,
                                          kind="Internal", addr_space="Local").ap()
        self.t["x2h"] = nc.dram_tensor("x2h", [TOK, H], F32, kind="Internal").ap()
        for tap, shape, dt in [
            ("tap_xhat", [128, KC * TOK], BF16),
            ("tap_q", [128, NH * TOK], BF16),
            ("tap_cc", [8 * NH, 128 * TOK], BF16),
            ("tap_ofm", [128, NH * TOK], BF16),
            ("tap_x2", [TOK, H], F32),
            ("tap_comb", [NE, TOK], BF16),
            ("tap_xhat2", [128, KC * TOK], BF16),
            ("tap_acc", [128, KC * TOK], F32),
            ("tap_kbuf", [128, 2 * 4096], BF16),
            ("tap_vtbuf", [128, 2 * 4096], BF16),
            ("tap_pa", [128, 4 * 2048], BF16),
            ("tap_pb", [128, 4 * 4096], BF16),
            ("tap_st5", [128, 256], F32),
            ("tap_rbs", [128, 512], BF16),
            ("tap_rrow", [1, 512], BF16),
            ("tap_lnst2", [128, 64], F32),
        ]:
            if tap in self.taps:
                self.t[tap] = nc.dram_tensor(tap, shape, dt, kind="ExternalOutput").ap()

    def _alloc_sbuf(self):
        nc = self.nc
        self.arena = nc.alloc_sbuf_tensor("arena", [128, ARENA], U8)
        self.abase = nc.lookup_mloc(self.arena).addr

        def P(name, shape, dtype, off):
            assert off % 32 == 0 and off + int(np.prod(shape[1:])) * mybir.dt.size(dtype) <= ARENA, name
            h = nc.virtual_sbuf_tensor(name, shape, dtype)
            nc._tensor(h.name, shape, dtype, type="SB")
            mloc = nc.lookup_mloc(h)
            mloc.allocated = True
            mloc.addr = self.abase + off
            self.t[name] = h

        P("x_tm", [128, NT * H], F32, A0)
        P("s_bv", [1, H], BF16, A0)
        P("s_sq", [ROT, 1024], F32, A0 + 8 * KB)
        P("s_rt", [ROT, 1024], F32, A0 + 12 * KB)
        P("s_rt2", [ROT, 1024], F32, A0 + 16 * KB)
        P("masks", [128, 48 * 512], BF16, A0)
        P("p_a", [128, 4 * 2048], BF16, A0 + 48 * KB)
        P("lg_s", [NE, TOK], F32, A0)
        P("lt_s", [128, NT * NE], F32, A0 + 4 * KB)
        P("pr_s", [128, NT * NE], F32, A0 + 6 * KB)
        P("pr2_s", [128, NT * NE], F32, A0 + 8 * KB)
        P("tv_s", [128, NT * 8], F32, A0 + 10 * KB)
        P("cum_s", [128, NT * 8], F32, A0 + 11 * KB)
        P("keep_s", [128, NT * 8], F32, A0 + 12 * KB)
        P("rstat", [128, NT * 8], F32, A0 + 13 * KB)
        P("cb_s", [128, NT * NE], BF16, A0 + 14 * KB)
        P("acc", [128, KC * TOK], F32, A0)
        P("xhat", [128, KC * TOK], BF16, B0)
        P("p_b", [128, 4 * 4096], BF16, B0)
        P("qfm", [128, NH * TOK], BF16, C0)
        P("ao", [128, NH * TOK], BF16, C0)
        P("wgu_b", [128, 2 * 6144], BF16, C0)
        P("yt9", [128, 2 * H], F32, C0)
        P("x2t9", [128, 2 * H], F32, C0 + 16 * KB)
        P("xh_tm", [128, H], BF16, D0)
        P("lnst", [128, NT * 8], F32, D0 + 4 * KB)
        P("kstage", [128, 2 * 1024], BF16, D0 + 6 * KB)
        P("xsq", [128, H], F32, D0 + 6 * KB)
        P("vstage", [128, 2 * 2048], BF16, D0 + 10 * KB)
        P("ofm", [128, NH * TOK], BF16, D0)
        P("wd_b", [128, 2 * 6144], BF16, D0)
        P("wv_b", [128, 2 * KC * 256], BF16, E0)
        P("wqk_b", [128, 2 * KC * 128], BF16, E0 + 16 * KB)
        P("cos_s", [ROT, TOK], BF16, E0 + 24 * KB)
        P("sin_s", [ROT, TOK], BF16, E0 + 26 * KB)
        P("wo_b", [128, 2 * 2048], BF16, E0)
        P("kbuf", [128, 2 * 4096], BF16, E0)
        P("vtbuf", [128, 2 * 4096], BF16, E0 + 16 * KB)
        P("st5", [128, 256], F32, E0 + 32 * KB)
        P("pt_s", [128, 2 * 512], BF16, E0 + 33 * KB)
        P("rb_s", [128, 512], BF16, E0 + 35 * KB + 512)
        P("rrow", [1, 512], BF16, E0 + 36 * KB + 512)
        P("kidx_s", [128, 512], F32, E0 + 37 * KB)
        P("dtab_s", [128, 48], F32, E0 + 39 * KB)
        P("xt6", [128, 2 * H], F32, E0 + 8 * KB)
        P("xh2_tm", [128, H], BF16, E0 + 24 * KB)
        P("lnst2", [128, NT * 8], F32, E0 + 28 * KB)
        P("xsq6", [128, H], F32, E0 + 29 * KB)
        P("h0", [128, 6 * 1024], BF16, E0)
        P("h1", [128, 6 * 1024], BF16, E0 + 12 * KB)
        P("silu_s", [128, 2 * 1024], BF16, E0 + 24 * KB)
        P("comb", [NE, TOK], BF16, E0 + 28 * KB)
        P("ce_b", [128, TOK], BF16, E0 + 30 * KB)
        P("sel_s", [NE, NE * 128], BF16, E0 + 32 * KB)
        P("id_bf", [128, 128], BF16, F0)
        P("id_f", [128, 128], F32, F0 + 256)
        P("smat_s", [128, ROT], BF16, F0 + 768)
        P("gate_s", [128, KC * NE], BF16, F0 + 1 * KB)
        P("bqk_s", [128, 32], F32, F0 + 2 * KB)
        P("bout_s", [128, 16], F32, F0 + 2 * KB + 256)
        P("ones_s", [1, 128], BF16, F0 + 2 * KB + 512)

    def _alloc_psum(self):
        nc = self.nc
        for n, shape, dt in [("psA", [128, 512], F32), ("psB", [128, 512], F32),
                             ("psC", [128, 2048], F32),
                             ("psT1", [128, 512], BF16), ("psT2", [128, 512], BF16)]:
            self.t[n] = nc.alloc_psum_tensor(n, shape, dt)

    def body(self, eng, fn):
        getattr(self.blk, eng)(fn)

    def _emit(self):
        self._p0()
        self._p1()
        if self.stage >= 2:
            self._p2()
        if self.stage >= 3:
            self._p3()
        if self.stage >= 5:
            self._p5()
        if self.stage >= 6:
            self._p6()
        if self.stage >= 7:
            self._p7()
        if self.stage >= 8:
            self._p8()
        if self.stage >= 9:
            self._p9()
        self._finish()

    def _p0(self):
        t, sem = self.t, self.sem
        ldc, ldg = sem["ldc"], sem["ldg"]

        ldm = self.sem["ldm"]

        def sync(s):
            for name, dst in [("kidx", "kidx_s"), ("dtab", "dtab_s")]:
                ldm.inc(s.dma_start(out=t[dst][:], in_=t[name]), 16)
            for name, dst in [("costab", "cos_s"), ("sintab", "sin_s"),
                              ("ident_bf", "id_bf"), ("ident_f", "id_f"),
                              ("smat", "smat_s"), ("bqk", "bqk_s"),
                              ("bout", "bout_s"), ("ones_bf", "ones_s")]:
                ldc.inc(s.dma_start(out=t[dst][:], in_=t[name]), 16)
        self.body("sync", sync)

        def gp(g):
            ldg.inc(g.dma_start(out=t["gate_s"][:], in_=t["gate"]), 16)
        self.body("gpsimd", gp)

    def _p1(self):
        t, sem = self.t, self.sem
        ldx, ln1, tr1, xh1 = sem["ldx"], sem["ln1"], sem["tr1"], sem["xh1"]

        def sync(s):
            xr = t["xin"].rearrange("(t p) c -> p t c", p=128)
            ldx.inc(s.dma_start(
                out=t["x_tm"][:].rearrange("p (t c) -> p t c", t=NT), in_=xr), 16)
        self.body("sync", sync)

        def vec(v):
            lnst = t["lnst"]
            ldx.wait(v, 16)
            for tt in range(NT):
                xt = t["x_tm"][:, tt * H:(tt + 1) * H]
                v.tensor_reduce(out=lnst[:, 8 * tt:8 * tt + 1], in_=xt, axis=AX.X,
                                op=ALU.add)
                if tt >= 1:
                    sem["sq1"].wait(v, tt)
                inst = v.scalar_tensor_tensor(out=t["xsq"][:], in0=xt, scalar=1.0,
                                              in1=xt, op0=ALU.mult, op1=ALU.mult,
                                              accum_out=lnst[:, 8 * tt + 1:8 * tt + 2])
                sem["sq1"].inc(inst)
            for tt in range(NT):
                v.tensor_scalar(out=lnst[:, 8 * tt + 2:8 * tt + 3],
                                in0=lnst[:, 8 * tt:8 * tt + 1], scalar1=1.0 / H,
                                scalar2=None, op0=ALU.mult)
                v.tensor_scalar(out=lnst[:, 8 * tt + 3:8 * tt + 4],
                                in0=lnst[:, 8 * tt + 1:8 * tt + 2], scalar1=1.0 / H,
                                scalar2=None, op0=ALU.mult)
            v.drain()
            for tt in range(NT):
                m = lnst[:, 8 * tt + 2:8 * tt + 3]
                v.tensor_tensor(out=lnst[:, 8 * tt + 4:8 * tt + 5], in0=m, in1=m,
                                op=ALU.mult)
            for tt in range(NT):
                inst = v.tensor_scalar(out=lnst[:, 8 * tt + 5:8 * tt + 6],
                                       in0=lnst[:, 8 * tt + 3:8 * tt + 4],
                                       scalar1=lnst[:, 8 * tt + 4:8 * tt + 5],
                                       scalar2=LN_EPS, op0=ALU.subtract, op1=ALU.add)
                if tt == NT - 1:
                    ln1.inc(inst)
        self.body("vector", vec)

        def sc(s):
            ln1.wait(s, 1)
            ls = t["lnst"].rearrange("p (t k) -> p t k", k=8)
            inst = s.activation(out=ls[:, :, 6:7], in_=ls[:, :, 5:6], func=AF.Sqrt)
            ln1.inc(inst)
        self.body("scalar", sc)

        def vec2(v):
            lnst = t["lnst"]
            ln1.wait(v, 2)
            ls = lnst.rearrange("p (t k) -> p t k", k=8)
            v.reciprocal(out=ls[:, :, 7:8], in_=ls[:, :, 6:7])
            v.drain()
            xhat4 = t["xhat"].rearrange("p (k t c) -> p k t c", k=KC, t=NT)
            for tt in range(NT):
                xt = t["x_tm"][:, tt * H:(tt + 1) * H]
                inst = v.tensor_scalar(out=t["xh_tm"][:], in0=xt,
                                       scalar1=lnst[:, 8 * tt + 2:8 * tt + 3],
                                       scalar2=lnst[:, 8 * tt + 7:8 * tt + 8],
                                       op0=ALU.subtract, op1=ALU.mult)
                ln1.inc(inst)
                for pk in range(4):
                    g = tt * 4 + pk
                    ps = t["psT1"] if pk % 2 == 0 else t["psT2"]
                    tr1.wait(v, g + 1)
                    inst = v.tensor_copy(out=xhat4[:, pk * 4:pk * 4 + 4, tt, :],
                                         in_=ps[:].rearrange("p (k c) -> p k c", k=4))
                    xh1.inc(inst)
        self.body("vector", vec2)

        def te(te_):
            for tt in range(NT):
                ln1.wait(te_, 3 + tt)
                for pk in range(4):
                    g = tt * 4 + pk
                    ps = t["psT1"] if pk % 2 == 0 else t["psT2"]
                    if g >= 2:
                        xh1.wait(te_, g - 1)
                    for i in range(4):
                        kc = pk * 4 + i
                        inst = te_.transpose(ps[:, 128 * i:128 * (i + 1)],
                                             t["xh_tm"][:, 128 * kc:128 * (kc + 1)],
                                             t["id_bf"][:])
                    tr1.inc(inst)
        self.body("tensor", te)

    def _p2(self):
        t, sem = self.t, self.sem
        wqkL = (sem["wqkL0"], sem["wqkL1"])
        wvL = (sem["wvL0"], sem["wvL1"])
        qkps, qkev = sem["qkps"], sem["qkev"]
        sqps, sqev, rope = sem["sqps"], sem["sqev"], sem["rope"]
        ccw = (sem["ccw0"], sem["ccw1"])
        ccwV = (sem["ccwV0"], sem["ccwV1"])
        vtps, vtev = sem["vtps"], sem["vtev"]
        xh1 = sem["xh1"]
        psAB = (t["psA"], t["psB"])

        def gp(g):
            sem["ln1"].wait(g, 2 + NT)
            sem["ldg"].inc(g.dma_start(out=t["s_bv"][:], in_=t["bv"]), 16)
            for oc in range(16):
                if oc >= 2:
                    qkev.wait(g, 2 * (oc - 1))
                wqkL[oc % 2].inc(g.dma_start(
                    out=t["wqk_b"][:, (oc % 2) * 2048:(oc % 2 + 1) * 2048],
                    in_=t["wqk"][oc]), 16)
            for vg in range(8):
                if vg >= 2:
                    vtev.wait(g, 8 * (vg - 1))
                wvL[vg % 2].inc(g.dma_start(
                    out=t["wv_b"][:, (vg % 2) * 4096:(vg % 2 + 1) * 4096],
                    in_=t["wv"][vg]), 16)
            for oc in range(16, 32):
                qkev.wait(g, 2 * (oc - 1))
                wqkL[oc % 2].inc(g.dma_start(
                    out=t["wqk_b"][:, (oc % 2) * 2048:(oc % 2 + 1) * 2048],
                    in_=t["wqk"][oc]), 16)
        self.body("gpsimd", gp)

        def qk_chunk_tensor(te_, oc):
            wqkL[oc % 2].wait(te_, 16 * (oc // 2 + 1))
            if oc == 0:
                xh1.wait(te_, 32)
                sem["ldc"].wait(te_, 128)
            for half in range(2):
                ps = psAB[half]
                idx = oc * 2 + half
                if idx >= 2:
                    qkev.wait(te_, idx - 1)
                for kc in range(KC):
                    inst = te_.matmul(
                        ps[:],
                        t["wqk_b"][:, (oc % 2) * 2048 + kc * 128:
                                   (oc % 2) * 2048 + (kc + 1) * 128],
                        t["xhat"][:, kc * TOK + half * 512:kc * TOK + (half + 1) * 512],
                        start=(kc == 0), stop=(kc == KC - 1))
                qkps.inc(inst)
            qkev.wait(te_, (oc + 1) * 2)
            if oc >= 1:
                sqev.wait(te_, oc)
            src = (t["kstage"][:, (oc % 2) * 1024:(oc % 2) * 1024 + 1024] if oc < 16
                   else t["qfm"][:, (oc - 16) * TOK:(oc - 15) * TOK])
            for half in range(2):
                inst = te_.matmul(
                    t["psC"][0:ROT, 1024 + half * 512:1024 + (half + 1) * 512],
                    t["smat_s"][:], src[:, half * 512:(half + 1) * 512],
                    start=True, stop=True)
            sqps.inc(inst)

        def qk_chunk_scalar(s, oc):
            if oc == 0:
                sem["ldc"].wait(s, 128)
            for half in range(2):
                idx = oc * 2 + half
                qkps.wait(s, idx + 1)
                if oc < 16:
                    if oc >= 2:
                        ccw[oc % 2].wait(s, 16 * (oc // 2))
                    dst = t["kstage"][:, (oc % 2) * 1024 + half * 512:
                                      (oc % 2) * 1024 + (half + 1) * 512]
                else:
                    dst = t["qfm"][:, (oc - 16) * TOK + half * 512:
                                   (oc - 16) * TOK + (half + 1) * 512]
                inst = s.activation(out=dst, in_=psAB[half][:], func=AF.Identity,
                                    bias=t["bqk_s"][:, oc:oc + 1], scale=1.0)
                qkev.inc(inst)
            sqps.wait(s, oc + 1)
            if oc >= 1:
                rope.wait(s, oc * 3)
            inst = s.activation(out=t["s_sq"][:], in_=t["psC"][0:ROT, 1024:2048],
                                func=AF.Identity)
            sqev.inc(inst)

        def qk_chunk_vector(v, oc):
            if oc == 0:
                sem["ldc"].wait(v, 128)
            sqev.wait(v, oc + 1)
            dst = (t["kstage"][:, (oc % 2) * 1024:(oc % 2) * 1024 + 1024] if oc < 16
                   else t["qfm"][:, (oc - 16) * TOK:(oc - 15) * TOK])
            rot = dst[0:ROT, :]
            v.tensor_tensor(out=t["s_rt"][:], in0=rot, in1=t["cos_s"][:], op=ALU.mult)
            v.tensor_tensor(out=t["s_rt2"][:], in0=t["s_sq"][:], in1=t["sin_s"][:],
                            op=ALU.mult)
            inst = v.tensor_tensor(out=rot, in0=t["s_rt"][:], in1=t["s_rt2"][:],
                                   op=ALU.add)
            rope.inc(inst, 3)

        def teK(te_):
            for oc in range(16):
                qk_chunk_tensor(te_, oc)
        self.body("tensor", teK)

        def scK(s):
            ldc.wait = None
        del scK

        def scK(s):
            for oc in range(16):
                qk_chunk_scalar(s, oc)
        self.body("scalar", scK)

        def veK(v):
            for oc in range(16):
                qk_chunk_vector(v, oc)
        self.body("vector", veK)

        def syK(s):
            for oc in range(16):
                rope.wait(s, (oc + 1) * 3)
                src = t["kstage"][:, (oc % 2) * 1024:(oc % 2) * 1024 + 1024]
                dst = t["cc_in"][2 * oc:2 * oc + 1, :].rearrange(
                    "a (p c) -> (a p) c", p=128)
                ccw[oc % 2].inc(s.dma_start(out=dst, in_=src), 16)
        self.body("sync", syK)

        def teV(te_):
            sem["ldg"].wait(te_, 32)
            for vg in range(8):
                wvL[vg % 2].wait(te_, 16 * (vg // 2 + 1))
                for tt in range(NT):
                    idx = vg * NT + tt
                    bank = idx % 2
                    ps = t["psC"][:, bank * 512:bank * 512 + 256]
                    if idx >= 2:
                        vtev.wait(te_, idx - 1)
                    for kc in range(KC):
                        te_.matmul(
                            ps,
                            t["xhat"][:, kc * TOK + tt * 128:kc * TOK + (tt + 1) * 128],
                            t["wv_b"][:, (vg % 2) * 4096 + kc * 256:
                                      (vg % 2) * 4096 + (kc + 1) * 256],
                            start=(kc == 0), stop=False)
                    inst = te_.matmul(ps, t["ones_s"][:],
                                      t["s_bv"][0:1, vg * 256:(vg + 1) * 256],
                                      start=False, stop=True)
                    vtps.inc(inst)
        self.body("tensor", teV)

        def scV(s):
            for vg in range(8):
                for tt in range(NT):
                    idx = vg * NT + tt
                    bank = idx % 2
                    vtps.wait(s, idx + 1)
                    if vg >= 2 and tt == 0:
                        ccwV[vg % 2].wait(s, 32 * (vg // 2))
                    inst = s.activation(
                        out=t["vstage"][:, (vg % 2) * 2048 + tt * 256:
                                        (vg % 2) * 2048 + (tt + 1) * 256],
                        in_=t["psC"][:, bank * 512:bank * 512 + 256],
                        func=AF.Identity)
                    vtev.inc(inst)
        self.body("scalar", scV)

        def syV(s):
            for vg in range(8):
                vtev.wait(s, (vg + 1) * NT)
                vst = t["vstage"].rearrange("p (pa t i c) -> p pa t i c",
                                            pa=2, t=NT, i=2)
                for i in range(2):
                    hh = vg * 2 + i
                    src = vst[:, vg % 2, :, i, :]
                    dst = t["cc_in"][2 * hh + 1:2 * hh + 2, :].rearrange(
                        "a (tt p c) -> a p tt c", tt=NT, p=128)[0]
                    ccwV[vg % 2].inc(s.dma_start(out=dst, in_=src), 16)
        self.body("sync", syV)

        def teQ(te_):
            for oc in range(16, 32):
                qk_chunk_tensor(te_, oc)
            sem["qdone"].inc(te_.nop())
        self.body("tensor", teQ)

        def scQ(s):
            for oc in range(16, 32):
                qk_chunk_scalar(s, oc)
        self.body("scalar", scQ)

        def veQ(v):
            for oc in range(16, 32):
                qk_chunk_vector(v, oc)
        self.body("vector", veQ)

    def _p3(self):
        t, sem = self.t, self.sem

        def gp(g):
            for nm in ("ccw0", "ccw1", "ccwV0", "ccwV1"):
                sem[nm].wait(g, 128)
            inst = g.collective_compute("AllGather", ALU.bypass,
                                        replica_groups=[[0, 1, 2, 3], [4, 5, 6, 7]],
                                        ins=[t["cc_in"][:]], outs=[t["cc_out"][:]])
            sem["ag"].inc(inst)
        self.body("gpsimd", gp)

    NB = (4, 8)

    def _p5(self):
        t, sem = self.t, self.sem
        mks, rope = sem["mks"], sem["rope"]

        def vmask(v):
            rope.wait(v, 96)
            sem["ldc"].wait(v, 32)
            for idx in range(48):
                inst = v.tensor_scalar(out=t["masks"][:, 512 * idx:512 * (idx + 1)],
                                       in0=t["kidx_s"][:],
                                       scalar1=t["dtab_s"][:, idx:idx + 1],
                                       scalar2=-1.0e30, op0=ALU.is_gt, op1=ALU.mult)
            mks.inc(inst)
        self.body("vector", vmask)

        for h in range(NH):
            self._p5_head(h)

    def _p5_head(self, h):
        t, sem = self.t, self.sem
        qk5, rx5, ex5, f5, fx5 = sem["qk5"], sem["rx5"], sem["ex5"], sem["f5"], sem["fx5"]
        sx5, pt5, ptev, pv5 = sem["sx5"], sem["pt5"], sem["ptev"], sem["pv5"]
        rb5, rbe, oev = sem["rb5"], sem["rbe"], sem["oev"]
        kvL = (sem["kvL0"], sem["kvL1"])
        rope = sem["rope"]
        par = h % 2
        sb = par * 128
        NBv = self.NB
        C_NMB, C_DIF, C_F, C_SBK, C_NM, C_S, C_LNZ, C_RIV = 0, 8, 16, 24, 32, 33, 34, 40
        st5 = t["st5"]

        def stc(c):
            return st5[:, sb + c:sb + c + 1]

        qkb = 48 * h
        fxb = 48 * h
        ptb = 48 * h
        pvb = 48 * h
        kvb = 128 * (h // 2)

        def pos_rh(b):
            r = min(b, 7 - b)
            return r, (0 if b < 4 else 1)

        def kcol(b):
            r, hb = pos_rh(b)
            return 1024 * r + 512 * hb

        def vtcol(b, i):
            r, hb = pos_rh(b)
            return 128 * (8 * r + 4 * hb + i)

        pbuf = (t["p_a"], t["p_b"])
        pstr = (2048, 4096)
        psAB = (t["psA"], t["psB"])

        def sync(s):
            if h <= 1:
                sem["ag"].wait(s, 1)
                sem["qkev"].wait(s, 64)
                sem["rope"].wait(s, 96)
            else:
                pv5.wait(s, 48 * (h - 1))
            for r in range(4):
                src = t["cc_out"][32 * r + 2 * h:32 * r + 2 * h + 1, :] \
                    .rearrange("a (p c) -> (a p) c", p=128)
                kvL[par].inc(s.dma_start(
                    out=t["kbuf"][:, par * 4096 + 1024 * r:par * 4096 + 1024 * (r + 1)],
                    in_=src), 16)
            for r in range(4):
                src = t["cc_out"][32 * r + 2 * h + 1:32 * r + 2 * h + 2, :] \
                    .rearrange("a (tt p c) -> a p tt c", tt=NT, p=128)[0]
                dst = t["vtbuf"].rearrange("p (pa tt c) -> p pa tt c", pa=2, tt=32) \
                    [:, par, 8 * r:8 * r + 8, :]
                kvL[par].inc(s.dma_start(out=dst, in_=src), 16)
        self.body("sync", sync)

        def te(te_):
            kvL[par].wait(te_, kvb + 128)
            rope.wait(te_, 3 * (17 + h))
            if h >= 1:
                rbe.wait(te_, 4 * h)
            cyc = qkb
            for g in range(2):
                nb = NBv[g]
                for tq in range(4):
                    qs = t["qfm"][:, h * 1024 + g * 512 + 128 * tq:
                                  h * 1024 + g * 512 + 128 * (tq + 1)]
                    for b in range(nb):
                        if cyc >= 4:
                            ex5.wait(te_, cyc - 3)
                        bk = cyc % 4
                        inst = te_.matmul(
                            t["psC"][:, 512 * bk:512 * (bk + 1)],
                            qs, t["kbuf"][:, par * 4096 + kcol(b):par * 4096 + kcol(b) + 512],
                            start=True, stop=True)
                        qk5.inc(inst)
                        cyc += 1
            ptc = ptb
            pvc = pvb
            for g in range(2):
                nb = NBv[g]
                n = 4 * nb
                fxbase = fxb + (13 if g == 0 else 41)
                pb = pbuf[g]

                def emit_T(kk):
                    b, i = kk // 4, kk % 4
                    ps = t["psT1"] if kk % 2 == 0 else t["psT2"]
                    fx5.wait(te_, fxbase + b)
                    for tq in range(4):
                        inst = te_.transpose(
                            ps[:, 128 * tq:128 * (tq + 1)],
                            pb[:, tq * pstr[g] + 512 * b + 128 * i:
                               tq * pstr[g] + 512 * b + 128 * (i + 1)],
                            t["id_bf"][:])
                    pt5.inc(inst)

                emit_T(0)
                emit_T(1)
                for kk in range(n):
                    b, i = kk // 4, kk % 4
                    ptev.wait(te_, ptc + kk + 1)
                    if kk == 0 and h >= 1:
                        oev.wait(te_, 2 * (h - 1) + g + 1)
                    inst = te_.matmul(
                        psAB[g][:],
                        t["vtbuf"][:, par * 4096 + vtcol(b, i):par * 4096 + vtcol(b, i) + 128],
                        t["pt_s"][:, (kk % 2) * 512:(kk % 2 + 1) * 512],
                        start=(kk == 0), stop=(kk == n - 1))
                    pv5.inc(inst)
                    if kk + 2 < n:
                        emit_T(kk + 2)
                ptc += n
                pvc += n
            for g in range(2):
                sem["rv5"].wait(te_, 8 * h + 4 * g + 4)
                if g == 1:
                    rbe.wait(te_, 4 * h + 2)
                for tq in range(4):
                    c = sb + C_RIV + 4 * g + tq
                    inst = te_.transpose(t["psC"][0:1, 128 * tq:128 * (tq + 1)],
                                         st5[:, c:c + 1], t["id_f"][:])
                rb5.inc(inst)
                rbe.wait(te_, 4 * h + 2 * g + 1)
                for tq in range(4):
                    inst = te_.matmul(
                        t["psC"][:, 512 + 128 * tq:512 + 128 * (tq + 1)],
                        t["ones_s"][:], t["rrow"][0:1, 128 * tq:128 * (tq + 1)],
                        start=True, stop=True)
                rb5.inc(inst)
        self.body("tensor", te)

        def ve(v):
            if h == 0:
                sem["mks"].wait(v, 1)
            cyc = qkb
            fxc = fxb
            for g in range(2):
                nb = NBv[g]
                pb = pbuf[g]
                for tq in range(4):
                    for b in range(nb):
                        qk5.wait(v, cyc + 1)
                        bk = cyc % 4
                        slot = 16 * g + tq * nb + b
                        v.tensor_tensor(out=t["psC"][:, 512 * bk:512 * (bk + 1)],
                                        in0=t["psC"][:, 512 * bk:512 * (bk + 1)],
                                        in1=t["masks"][:, 512 * slot:512 * (slot + 1)],
                                        op=ALU.add)
                        inst = v.tensor_reduce(
                            out=st5[:, sb + C_NMB + b:sb + C_NMB + b + 1],
                            in_=t["psC"][:, 512 * bk:512 * (bk + 1)],
                            axis=AX.X, op=ALU.max, negate=True)
                        rx5.inc(inst)
                        cyc += 1
                    v.tensor_reduce(out=stc(C_NM),
                                    in_=st5[:, sb + C_NMB:sb + C_NMB + nb],
                                    axis=AX.X, op=ALU.min)
                    v.drain()
                    inst = v.tensor_scalar(out=st5[:, sb + C_DIF:sb + C_DIF + nb],
                                           in0=st5[:, sb + C_NMB:sb + C_NMB + nb],
                                           scalar1=stc(C_NM), scalar2=None,
                                           op0=ALU.subtract)
                    sem["df5"].inc(inst)
                    for b in range(nb):
                        f5.wait(v, 8 * h + 4 * g + tq + 1)
                        sl = pb[:, tq * pstr[g] + 512 * b:tq * pstr[g] + 512 * (b + 1)]
                        inst = v.tensor_scalar(
                            out=sl, in0=sl,
                            scalar1=st5[:, sb + C_F + b:sb + C_F + b + 1],
                            scalar2=None, op0=ALU.mult)
                        fx5.inc(inst)
                    inst = v.tensor_reduce(
                        out=stc(C_S),
                        in_=pb[:, tq * pstr[g]:tq * pstr[g] + 512 * nb],
                        axis=AX.X, op=ALU.add)
                    sx5.inc(inst)
            for g in range(2):
                rb5.wait(v, 4 * h + 2 * g + 2)
                inst = v.tensor_copy(out=t["rb_s"][:], in_=t["psC"][:, 512:1024])
                rbe.inc(inst)
                pv5.wait(v, pvb + (16 if g == 0 else 48))
                inst = v.tensor_tensor(out=t["ofm"][:, h * 1024 + 512 * g:
                                                    h * 1024 + 512 * (g + 1)],
                                       in0=psAB[g][:], in1=t["rb_s"][:], op=ALU.mult)
                oev.inc(inst)
        self.body("vector", ve)

        def sc(s):
            cyc = qkb
            for g in range(2):
                nb = NBv[g]
                pb = pbuf[g]
                for tq in range(4):
                    for b in range(nb):
                        rx5.wait(s, cyc + 1)
                        bk = cyc % 4
                        inst = s.activation(
                            out=pb[:, tq * pstr[g] + 512 * b:tq * pstr[g] + 512 * (b + 1)],
                            in_=t["psC"][:, 512 * bk:512 * (bk + 1)],
                            func=AF.Exp,
                            bias=st5[:, sb + C_NMB + b:sb + C_NMB + b + 1], scale=1.0)
                        ex5.inc(inst)
                        cyc += 1
                    sem["df5"].wait(s, 8 * h + 4 * g + tq + 1)
                    inst = s.activation(out=st5[:, sb + C_F:sb + C_F + nb],
                                        in_=st5[:, sb + C_DIF:sb + C_DIF + nb],
                                        func=AF.Exp, bias=0.0, scale=-1.0)
                    f5.inc(inst)
                    sx5.wait(s, 8 * h + 4 * g + tq + 1)
                    s.activation(out=stc(C_LNZ), in_=stc(C_S), func=AF.Ln)
                    inst = s.activation(out=st5[:, sb + C_RIV + 4 * g + tq:
                                                sb + C_RIV + 4 * g + tq + 1],
                                        in_=stc(C_LNZ), func=AF.Exp, bias=0.0, scale=-1.0)
                    sem["rv5"].inc(inst)
            ptc = ptb
            for g in range(2):
                n = 4 * NBv[g]
                for kk in range(n):
                    pt5.wait(s, ptc + kk + 1)
                    if ptc + kk >= 2:
                        pv5.wait(s, ptc + kk - 1)
                    ps = t["psT1"] if kk % 2 == 0 else t["psT2"]
                    inst = s.activation(out=t["pt_s"][:, (kk % 2) * 512:(kk % 2 + 1) * 512],
                                        in_=ps[:], func=AF.Identity)
                    ptev.inc(inst)
                ptc += n
            for g in range(2):
                rb5.wait(s, 4 * h + 2 * g + 1)
                inst = s.activation(out=t["rrow"][0:1, :], in_=t["psC"][0:1, 0:512],
                                    func=AF.Identity)
                rbe.inc(inst)
        self.body("scalar", sc)

    def _p6(self):
        t, sem = self.t, self.sem
        woL = (sem["woL0"], sem["woL1"])
        aops, aoev = sem["aops"], sem["aoev"]
        tr6, x2w, x2d, ln2 = sem["tr6"], sem["x2w"], sem["x2d"], sem["ln2"]
        xh2s, xh2, oev = sem["xh2s"], sem["xh2"], sem["oev"]
        ldx6 = (sem["ldx60"], sem["ldx61"])
        ln2a, ln2b = sem["ln2a"], sem["ln2b"]
        psAB = (t["psA"], t["psB"])

        def gp(g):
            oev.wait(g, 32)
            for oc in range(16):
                if oc >= 2:
                    aoev.wait(g, 2 * (oc - 1))
                woL[oc % 2].inc(g.dma_start(
                    out=t["wo_b"][:, (oc % 2) * 2048:(oc % 2 + 1) * 2048],
                    in_=t["wout"][oc]), 16)
        self.body("gpsimd", gp)

        def te(te_):
            oev.wait(te_, 32)
            for oc in range(16):
                woL[oc % 2].wait(te_, 16 * (oc // 2 + 1))
                for half in range(2):
                    idx = oc * 2 + half
                    if idx >= 2:
                        aoev.wait(te_, idx - 1)
                    for kc in range(KC):
                        inst = te_.matmul(
                            psAB[half][:],
                            t["wo_b"][:, (oc % 2) * 2048 + kc * 128:
                                      (oc % 2) * 2048 + (kc + 1) * 128],
                            t["ofm"][:, kc * TOK + half * 512:kc * TOK + (half + 1) * 512],
                            start=(kc == 0), stop=(kc == KC - 1))
                    aops.inc(inst)
        self.body("tensor", te)

        def sc(s):
            for oc in range(16):
                for half in range(2):
                    idx = oc * 2 + half
                    aops.wait(s, idx + 1)
                    inst = s.activation(
                        out=t["ao"][:, oc * TOK + half * 512:oc * TOK + (half + 1) * 512],
                        in_=psAB[half][:], func=AF.Identity,
                        bias=t["bout_s"][:, oc:oc + 1], scale=1.0)
                    aoev.inc(inst)
        self.body("scalar", sc)

        def sy(s):
            xr = t["xin"].rearrange("(t p) c -> t p c", p=128)
            x2r = t["x2h"].rearrange("(t p) c -> t p c", p=128)

            def load(tt):
                if tt < 2:
                    oev.wait(s, 32)
                if tt >= 2:
                    ln2.wait(s, tt - 1)
                    x2d.wait(s, 16 * (tt - 1))
                ldx6[tt % 2].inc(s.dma_start(
                    out=t["xt6"][:, (tt % 2) * H:(tt % 2 + 1) * H], in_=xr[tt]), 16)

            def store(tt):
                x2w.wait(s, 4 * (tt + 1))
                x2d.inc(s.dma_start(
                    out=x2r[tt], in_=t["xt6"][:, (tt % 2) * H:(tt % 2 + 1) * H]), 16)

            load(0)
            load(1)
            for tt in range(NT - 2):
                store(tt)
                load(tt + 2)
            store(NT - 2)
            store(NT - 1)
        self.body("sync", sy)

        def te2(te_):
            aoev.wait(te_, 32)
            evlist = []

            def guard():
                k = len(evlist)
                if k >= 2:
                    sem_, val = evlist[k - 2]
                    sem_.wait(te_, val)
                return t["psT1"] if k % 2 == 0 else t["psT2"]

            def aoT(tt):
                for grp in range(4):
                    ps = guard()
                    for i in range(4):
                        oc = grp * 4 + i
                        inst = te_.transpose(
                            ps[:, 128 * i:128 * (i + 1)],
                            t["ao"][:, oc * TOK + 128 * tt:oc * TOK + 128 * (tt + 1)],
                            t["id_bf"][:])
                    tr6.inc(inst)
                    evlist.append((x2w, 4 * tt + grp + 1))

            def xh2T(tt):
                ln2.wait(te_, tt + 1)
                for pk in range(4):
                    ps = guard()
                    for i in range(4):
                        kc = pk * 4 + i
                        inst = te_.transpose(
                            ps[:, 128 * i:128 * (i + 1)],
                            t["xh2_tm"][:, 128 * kc:128 * (kc + 1)],
                            t["id_bf"][:])
                    xh2s.inc(inst)
                    evlist.append((xh2, 4 * tt + pk + 1))

            for tt in range(NT):
                aoT(tt)
                xh2T(tt)
        self.body("tensor", te2)

        def sc2(s):
            for tt in range(NT):
                ln2a.wait(s, tt + 1)
                inst = s.activation(out=t["lnst2"][:, 8 * tt + 6:8 * tt + 7],
                                    in_=t["lnst2"][:, 8 * tt + 5:8 * tt + 6],
                                    func=AF.Sqrt)
                ln2b.inc(inst)
        self.body("scalar", sc2)

        def ve(v):
            lnst = t["lnst2"]
            xhat4 = t["xhat"].rearrange("p (k t c) -> p k t c", k=KC, t=NT)
            order = []
            for _tt in range(NT):
                order += [("a", _tt, g) for g in range(4)]
                order += [("h", _tt, g) for g in range(4)]
            par6 = {key: i % 2 for i, key in enumerate(order)}
            for tt in range(NT):
                xs = t["xt6"][:, (tt % 2) * H:(tt % 2 + 1) * H]
                ldx6[tt % 2].wait(v, 16 * (tt // 2 + 1))
                for grp in range(4):
                    gidx = tt * 4 + grp
                    tr6.wait(v, gidx + 1)
                    ps = t["psT1"] if par6[("a", tt, grp)] == 0 else t["psT2"]
                    inst = v.scalar_tensor_tensor(
                        out=xs[:, 512 * grp:512 * (grp + 1)],
                        in0=ps[:], scalar=1.0,
                        in1=xs[:, 512 * grp:512 * (grp + 1)],
                        op0=ALU.bypass, op1=ALU.add)
                    x2w.inc(inst)
                v.tensor_reduce(out=lnst[:, 8 * tt:8 * tt + 1], in_=xs, axis=AX.X,
                                op=ALU.add)
                if tt >= 1:
                    sem["sq6"].wait(v, tt)
                inst = v.scalar_tensor_tensor(out=t["xsq6"][:], in0=xs, scalar=1.0,
                                              in1=xs, op0=ALU.mult, op1=ALU.mult,
                                              accum_out=lnst[:, 8 * tt + 1:8 * tt + 2])
                sem["sq6"].inc(inst)
                v.tensor_scalar(out=lnst[:, 8 * tt + 2:8 * tt + 3],
                                in0=lnst[:, 8 * tt:8 * tt + 1], scalar1=1.0 / H,
                                scalar2=None, op0=ALU.mult)
                v.tensor_scalar(out=lnst[:, 8 * tt + 3:8 * tt + 4],
                                in0=lnst[:, 8 * tt + 1:8 * tt + 2], scalar1=1.0 / H,
                                scalar2=None, op0=ALU.mult)
                v.drain()
                m = lnst[:, 8 * tt + 2:8 * tt + 3]
                v.tensor_tensor(out=lnst[:, 8 * tt + 4:8 * tt + 5], in0=m, in1=m,
                                op=ALU.mult)
                v.drain()
                inst = v.tensor_scalar(out=lnst[:, 8 * tt + 5:8 * tt + 6],
                                       in0=lnst[:, 8 * tt + 3:8 * tt + 4],
                                       scalar1=lnst[:, 8 * tt + 4:8 * tt + 5],
                                       scalar2=LN_EPS, op0=ALU.subtract, op1=ALU.add)
                ln2a.inc(inst)
                ln2b.wait(v, tt + 1)
                v.reciprocal(out=lnst[:, 8 * tt + 7:8 * tt + 8],
                             in_=lnst[:, 8 * tt + 6:8 * tt + 7])
                v.drain()
                inst = v.tensor_scalar(out=t["xh2_tm"][:], in0=xs,
                                       scalar1=lnst[:, 8 * tt + 2:8 * tt + 3],
                                       scalar2=lnst[:, 8 * tt + 7:8 * tt + 8],
                                       op0=ALU.subtract, op1=ALU.mult)
                ln2.inc(inst)
                for pk in range(4):
                    xh2s.wait(v, tt * 4 + pk + 1)
                    ps = t["psT1"] if par6[("h", tt, pk)] == 0 else t["psT2"]
                    inst = v.tensor_copy(out=xhat4[:, pk * 4:pk * 4 + 4, tt, :],
                                         in_=ps[:].rearrange("p (k c) -> p k c", k=4))
                    xh2.inc(inst)
        self.body("vector", ve)

    def _p7(self):
        t, sem = self.t, self.sem
        gps, gev, rt7, cmb = sem["gps"], sem["gev"], sem["rt7"], sem["cmb"]
        r7a, r7b = sem["r7a"], sem["r7b"]
        xh2, ldg = sem["xh2"], sem["ldg"]
        BIG = 1.0e30

        def sy(s):
            xh2.wait(s, 32)
            sem["ldc"].inc(s.dma_start(out=t["sel_s"][:], in_=t["selmat"]), 16)
        self.body("sync", sy)

        def te(te_):
            xh2.wait(te_, 32)
            ldg.wait(te_, 32)
            for half in range(2):
                for kc in range(KC):
                    inst = te_.matmul(
                        t["psC"][0:NE, half * 512:(half + 1) * 512],
                        t["gate_s"][:, kc * NE:(kc + 1) * NE],
                        t["xhat"][:, kc * TOK + half * 512:kc * TOK + (half + 1) * 512],
                        start=(kc == 0), stop=(kc == KC - 1))
                gps.inc(inst)
            gev.wait(te_, 2)
            for tt in range(NT):
                ps = t["psA"] if tt < 4 else t["psB"]
                inst = te_.transpose(ps[:, 32 * (tt % 4):32 * (tt % 4) + 32],
                                     t["lg_s"][:, 128 * tt:128 * (tt + 1)],
                                     t["id_f"][0:NE, 0:NE])
                if tt % 4 == 3:
                    rt7.inc(inst)
        self.body("tensor", te)

        def sc(s):
            for half in range(2):
                gps.wait(s, half + 1)
                inst = s.activation(out=t["lg_s"][:, half * 512:(half + 1) * 512],
                                    in_=t["psC"][0:NE, half * 512:(half + 1) * 512],
                                    func=AF.Identity)
                gev.inc(inst)
        self.body("scalar", sc)

        def ve(v):
            rs = t["rstat"]

            def rc(tt, k):
                return rs[:, 8 * tt + k:8 * tt + k + 1]

            for i, ps in enumerate((t["psA"], t["psB"])):
                rt7.wait(v, i + 1)
                v.tensor_copy(out=t["lt_s"][:, 128 * i:128 * (i + 1)], in_=ps[:, 0:128])
            for tt in range(NT):
                inst = v.tensor_reduce(out=rc(tt, 0),
                                       in_=t["lt_s"][:, 32 * tt:32 * (tt + 1)],
                                       axis=AX.X, op=ALU.max, negate=True)
                r7a.inc(inst)
            for tt in range(NT):
                r7b.wait(v, tt + 1)
                v.tensor_reduce(out=rc(tt, 1), in_=t["pr_s"][:, 32 * tt:32 * (tt + 1)],
                                axis=AX.X, op=ALU.add)
            for tt in range(NT):
                v.reciprocal(out=rc(tt, 2), in_=rc(tt, 1))
            v.drain()
            for tt in range(NT):
                v.tensor_scalar(out=t["pr_s"][:, 32 * tt:32 * (tt + 1)],
                                in0=t["pr_s"][:, 32 * tt:32 * (tt + 1)],
                                scalar1=rc(tt, 2), scalar2=None, op0=ALU.mult)
            for tt in range(NT):
                v.tensor_copy(out=t["pr2_s"][:, 32 * tt:32 * (tt + 1)],
                              in_=t["pr_s"][:, 32 * tt:32 * (tt + 1)])
            for tt in range(NT):
                v.max(out=t["tv_s"][:, 8 * tt:8 * (tt + 1)],
                      in_=t["pr_s"][:, 32 * tt:32 * (tt + 1)])
            for tt in range(NT):
                v.tensor_tensor_scan(out=t["cum_s"][:, 8 * tt:8 * (tt + 1)],
                                     data0=t["tv_s"][:, 8 * tt:8 * (tt + 1)],
                                     data1=t["tv_s"][:, 8 * tt:8 * (tt + 1)],
                                     initial=0.0, op0=ALU.add, op1=ALU.bypass)
            for tt in range(NT):
                v.tensor_reduce(out=rc(tt, 3), in_=t["tv_s"][:, 8 * tt:8 * (tt + 1)],
                                axis=AX.X, op=ALU.add)
            for tt in range(NT):
                v.tensor_tensor(out=t["cum_s"][:, 8 * tt:8 * (tt + 1)],
                                in0=t["cum_s"][:, 8 * tt:8 * (tt + 1)],
                                in1=t["tv_s"][:, 8 * tt:8 * (tt + 1)], op=ALU.subtract)
            for tt in range(NT):
                v.tensor_scalar(out=rc(tt, 4), in0=rc(tt, 3), scalar1=0.8,
                                scalar2=None, op0=ALU.mult)
            for tt in range(NT):
                v.tensor_scalar(out=t["keep_s"][:, 8 * tt:8 * (tt + 1)],
                                in0=t["cum_s"][:, 8 * tt:8 * (tt + 1)],
                                scalar1=rc(tt, 4), scalar2=None, op0=ALU.is_lt)
            for tt in range(NT):
                v.scalar_tensor_tensor(out=t["cum_s"][:, 8 * tt:8 * (tt + 1)],
                                       in0=t["keep_s"][:, 8 * tt:8 * (tt + 1)],
                                       scalar=-BIG, in1=t["tv_s"][:, 8 * tt:8 * (tt + 1)],
                                       op0=ALU.mult, op1=ALU.add)
            for tt in range(NT):
                v.tensor_reduce(out=rc(tt, 5), in_=t["cum_s"][:, 8 * tt:8 * (tt + 1)],
                                axis=AX.X, op=ALU.min)
            for tt in range(NT):
                v.tensor_scalar(out=rc(tt, 5), in0=rc(tt, 5), scalar1=BIG,
                                scalar2=None, op0=ALU.add)
            for tt in range(NT):
                v.scalar_tensor_tensor(out=t["tv_s"][:, 8 * tt:8 * (tt + 1)],
                                       in0=t["tv_s"][:, 8 * tt:8 * (tt + 1)],
                                       scalar=1.0, in1=t["keep_s"][:, 8 * tt:8 * (tt + 1)],
                                       op0=ALU.bypass, op1=ALU.mult,
                                       accum_out=rc(tt, 6))
            for tt in range(NT):
                v.reciprocal(out=rc(tt, 7), in_=rc(tt, 6))
            v.drain()
            for tt in range(NT):
                v.tensor_scalar(out=t["pr_s"][:, 32 * tt:32 * (tt + 1)],
                                in0=t["pr2_s"][:, 32 * tt:32 * (tt + 1)],
                                scalar1=rc(tt, 5), scalar2=None, op0=ALU.is_ge)
            for tt in range(NT):
                v.tensor_tensor(out=t["pr_s"][:, 32 * tt:32 * (tt + 1)],
                                in0=t["pr_s"][:, 32 * tt:32 * (tt + 1)],
                                in1=t["pr2_s"][:, 32 * tt:32 * (tt + 1)], op=ALU.mult)
            for tt in range(NT):
                inst = v.tensor_scalar(out=t["cb_s"][:, 32 * tt:32 * (tt + 1)],
                                       in0=t["pr_s"][:, 32 * tt:32 * (tt + 1)],
                                       scalar1=rc(tt, 7), scalar2=None, op0=ALU.mult)
                r7a.inc(inst)
        self.body("vector", ve)

        def sc2(s):
            for tt in range(NT):
                r7a.wait(s, tt + 1)
                inst = s.activation(out=t["pr_s"][:, 32 * tt:32 * (tt + 1)],
                                    in_=t["lt_s"][:, 32 * tt:32 * (tt + 1)],
                                    func=AF.Exp,
                                    bias=t["rstat"][:, 8 * tt:8 * tt + 1], scale=1.0)
                r7b.inc(inst)
        self.body("scalar", sc2)

        def te2(te_):
            for grp in range(2):
                ps = t["psT1"] if grp == 0 else t["psT2"]
                r7a.wait(te_, 8 + 4 * (grp + 1))
                for i in range(4):
                    tt = grp * 4 + i
                    inst = te_.transpose(ps[0:NE, 128 * i:128 * (i + 1)],
                                         t["cb_s"][:, 32 * tt:32 * (tt + 1)],
                                         t["id_bf"][:])
                sem["ceps"].inc(inst)
        self.body("tensor", te2)

        def ve2(v):
            for grp in range(2):
                sem["ceps"].wait(v, grp + 1)
                ps = t["psT1"] if grp == 0 else t["psT2"]
                inst = v.tensor_copy(out=t["comb"][:, 512 * grp:512 * (grp + 1)],
                                     in_=ps[0:NE, :])
                cmb.inc(inst)
        self.body("vector", ve2)

    def _p8(self):
        t, sem = self.t, self.sem
        wguL = (sem["wguL0"], sem["wguL1"])
        wdL = (sem["wdL0"], sem["wdL1"])
        gups, guevS, guevV, hrd = sem["gups"], sem["guevS"], sem["guevV"], sem["hrd"]
        ceps, ceev, dps, dev = sem["ceps"], sem["ceev"], sem["dps"], sem["dev"]
        cmb = sem["cmb"]
        psAB = (t["psA"], t["psB"])

        def gp(g):
            def gu_load(e, q):
                gi = 4 * e + q
                if gi >= 2:
                    e2, q2 = (gi - 2) // 4, (gi - 2) % 4
                    guevS.wait(g, 12 * e2 + (4, 6, 10, 12)[q2])
                    guevV.wait(g, 12 * e2 + (2, 6, 8, 12)[q2])
                wguL[gi % 2].inc(g.dma_start(
                    out=t["wgu_b"][:, (gi % 2) * 6144:(gi % 2 + 1) * 6144],
                    in_=t["wgu"][e, q]), 16)

            def wd_load(pr, mg):
                gi = 4 * pr + mg
                if gi >= 2:
                    p2, m2 = (gi - 2) // 4, (gi - 2) % 4
                    sem["dev"].wait(g, 32 * p2 + 8 * (m2 + 1))
                wdL[gi % 2].inc(g.dma_start(
                    out=t["wd_b"][:, (gi % 2) * 6144:(gi % 2 + 1) * 6144],
                    in_=t["wd"][pr, mg]), 16)

            sem["tr6"].wait(g, 32)
            sem["aops"].wait(g, 32)
            for pr in range(NE // 2):
                for ei in range(2):
                    for q in range(4):
                        gu_load(2 * pr + ei, q)
                for mg in range(4):
                    wd_load(pr, mg)
        self.body("gpsimd", gp)

        def te(te_):
            sem["ldc"].wait(te_, sem["ldc"].n)
            cmb.wait(te_, 2)
            for pr in range(NE // 2):
                for ei in range(2):
                    e = 2 * pr + ei
                    ceev.wait(te_, e)
                    for half in range(2):
                        inst = te_.matmul(
                            t["psC"][:, 1024 + half * 512:1024 + (half + 1) * 512],
                            t["sel_s"][0:NE, 128 * e:128 * (e + 1)],
                            t["comb"][0:NE, half * 512:(half + 1) * 512],
                            start=True, stop=True)
                    ceps.inc(inst)
                    for mi in range(12):
                        gci, is_up = mi // 2, mi % 2
                        q, m3 = mi // 3, mi % 3
                        gi = 4 * e + q
                        wguL[gi % 2].wait(te_, 16 * (gi // 2 + 1))
                        for half in range(2):
                            if is_up:
                                ps = t["psC"][:, half * 512:(half + 1) * 512]
                                if not (e == 0 and gci == 0):
                                    pe, pg = (e, gci - 1) if gci > 0 else (e - 1, 5)
                                    guevV.wait(te_, 12 * pe + 2 * pg + half + 1)
                                    if gci == 0:
                                        sem["dev"].wait(te_, 32 * pr)
                            else:
                                ps = psAB[half][:]
                                if not (e == 0 and gci == 0):
                                    pe, pg = (e, gci - 1) if gci > 0 else (e - 1, 5)
                                    guevS.wait(te_, 12 * pe + 2 * pg + half + 1)
                                if not (e == 0 and gci < 2):
                                    se_, sg_ = (e, gci - 2) if gci >= 2 else (e - 1, gci + 4)
                                    hrd.wait(te_, 12 * se_ + 2 * sg_ + half + 1)
                            for kc in range(KC):
                                inst = te_.matmul(
                                    ps,
                                    t["wgu_b"][:, ((4 * e + q) % 2) * 6144 +
                                               (m3 * KC + kc) * 128:
                                               ((4 * e + q) % 2) * 6144 +
                                               (m3 * KC + kc) * 128 + 128],
                                    t["xhat"][:, kc * TOK + half * 512:
                                              kc * TOK + (half + 1) * 512],
                                    start=(kc == 0), stop=(kc == KC - 1))
                            gups.inc(inst)
                hrd.wait(te_, 12 * (pr + 1) * 2)
                for m in range(16):
                    mg, m4 = m // 4, m % 4
                    gid = 4 * pr + mg
                    wdL[gid % 2].wait(te_, 16 * (gid // 2 + 1))
                    for half in range(2):
                        idx_d = 32 * pr + m * 2 + half
                        ps = t["psC"][:, half * 512:(half + 1) * 512]
                        if idx_d >= 2:
                            sem["dev"].wait(te_, idx_d - 1)
                        nmm = 0
                        for ei in range(2):
                            for kc in range(6):
                                inst = te_.matmul(
                                    ps,
                                    t["wd_b"][:, ((4 * pr + mg) % 2) * 6144 +
                                              ((m4 * 2 + ei) * 6 + kc) * 128:
                                              ((4 * pr + mg) % 2) * 6144 +
                                              ((m4 * 2 + ei) * 6 + kc) * 128 + 128],
                                    (t["h0"], t["h1"])[ei][:, kc * TOK + half * 512:
                                                           kc * TOK + (half + 1) * 512],
                                    start=(ei == 0 and kc == 0),
                                    stop=(ei == 1 and kc == 5))
                        dps.inc(inst)
        self.body("tensor", te)

        def sc(s):
            for pr in range(NE // 2):
                for ei in range(2):
                    e = 2 * pr + ei
                    ceps.wait(s, 2 + e + 1)
                    if e >= 1:
                        guevV.wait(s, 12 * e)
                    inst = s.activation(out=t["ce_b"][:], in_=t["psC"][:, 1024:2048],
                                        func=AF.Identity)
                    ceev.inc(inst)
                    for gci in range(6):
                        for half in range(2):
                            gups.wait(s, 24 * e + gci * 4 + half + 1)
                            inst = s.activation(
                                out=t["silu_s"][:, (gci % 2) * 1024 + half * 512:
                                                (gci % 2) * 1024 + (half + 1) * 512],
                                in_=psAB[half][:], func=AF.Silu)
                            guevS.inc(inst)
        self.body("scalar", sc)

        def ve(v):
            for pr in range(NE // 2):
                for ei in range(2):
                    e = 2 * pr + ei
                    he = (t["h0"], t["h1"])[ei]
                    for gci in range(6):
                        for half in range(2):
                            gups.wait(v, 24 * e + gci * 4 + 2 + half + 1)
                            ceev.wait(v, e + 1)
                            sl = he[:, gci * TOK + half * 512:gci * TOK + (half + 1) * 512]
                            inst = v.scalar_tensor_tensor(
                                out=sl, in0=t["psC"][:, half * 512:(half + 1) * 512],
                                scalar=1.0,
                                in1=t["ce_b"][:, half * 512:(half + 1) * 512],
                                op0=ALU.bypass, op1=ALU.mult)
                            guevV.inc(inst)
                            guevS.wait(v, 12 * e + gci * 2 + half + 1)
                            inst = v.tensor_tensor(
                                out=sl, in0=sl,
                                in1=t["silu_s"][:, (gci % 2) * 1024 + half * 512:
                                                (gci % 2) * 1024 + (half + 1) * 512],
                                op=ALU.mult)
                            hrd.inc(inst)
                for m in range(16):
                    for half in range(2):
                        idx_d = 32 * pr + m * 2 + half
                        dps.wait(v, idx_d + 1)
                        dst = t["acc"][:, m * TOK + half * 512:m * TOK + (half + 1) * 512]
                        if pr == 0:
                            inst = v.tensor_copy(out=dst,
                                                 in_=t["psC"][:, half * 512:(half + 1) * 512])
                        else:
                            inst = v.tensor_tensor(out=dst, in0=dst,
                                                   in1=t["psC"][:, half * 512:(half + 1) * 512],
                                                   op=ALU.add)
                        sem["dev"].inc(inst)
        self.body("vector", ve)

    def _p9(self):
        t, sem = self.t, self.sem
        tr9, ytv, x2r, yst = sem["tr9"], sem["ytv"], sem["x2r"], sem["yst"]
        psAB = (t["psA"], t["psB"])

        def sy(s):
            x2v = t["x2h"].rearrange("(t p) c -> t p c", p=128)
            yv = t["y"].rearrange("(t p) c -> t p c", p=128)

            def load(tt):
                if tt < 2:
                    sem["dev"].wait(s, 512)
                if tt >= 2:
                    ytv.wait(s, 4 * (tt - 1))
                sem["x2d"].wait(s, 16 * (tt + 1))
                x2r.inc(s.dma_start(out=t["x2t9"][:, (tt % 2) * H:(tt % 2 + 1) * H],
                                    in_=x2v[tt]), 16)

            def store(tt):
                ytv.wait(s, 4 * (tt + 1))
                yst.inc(s.dma_start(out=yv[tt],
                                    in_=t["yt9"][:, (tt % 2) * H:(tt % 2 + 1) * H]), 16)

            load(0)
            load(1)
            for tt in range(NT - 2):
                store(tt)
                load(tt + 2)
            store(NT - 2)
            store(NT - 1)
        self.body("sync", sy)

        def te(te_):
            sem["dev"].wait(te_, 512)
            for tt in range(NT):
                for grp in range(4):
                    gidx = tt * 4 + grp
                    ps = psAB[gidx % 2]
                    if gidx >= 2:
                        ytv.wait(te_, gidx - 1)
                    for i in range(4):
                        kc = grp * 4 + i
                        inst = te_.transpose(
                            ps[:, 128 * i:128 * (i + 1)],
                            t["acc"][:, kc * TOK + 128 * tt:kc * TOK + 128 * (tt + 1)],
                            t["id_f"][:])
                    tr9.inc(inst)
        self.body("tensor", te)

        def ve(v):
            for tt in range(NT):
                x2r.wait(v, 16 * (tt + 1))
                if tt >= 2:
                    yst.wait(v, 16 * (tt - 1))
                for grp in range(4):
                    gidx = tt * 4 + grp
                    tr9.wait(v, gidx + 1)
                    ps = psAB[gidx % 2]
                    inst = v.tensor_tensor(
                        out=t["yt9"][:, (tt % 2) * H + 512 * grp:
                                     (tt % 2) * H + 512 * (grp + 1)],
                        in0=ps[:],
                        in1=t["x2t9"][:, (tt % 2) * H + 512 * grp:
                                      (tt % 2) * H + 512 * (grp + 1)],
                        op=ALU.add)
                    ytv.inc(inst)
        self.body("vector", ve)

    def _finish(self):
        t, sem = self.t, self.sem
        tap = sem["tap"]

        def sync(s):
            if "tap_xhat" in self.taps:
                sem["xh1"].wait(s, 32)
                tap.inc(s.dma_start(out=t["tap_xhat"], in_=t["xhat"][:]), 16)
            if "tap_q" in self.taps and self.stage >= 2:
                sem["rope"].wait(s, 96)
                tap.inc(s.dma_start(out=t["tap_q"], in_=t["qfm"][:]), 16)
            if "tap_cc" in self.taps and self.stage >= 3:
                sem["ag"].wait(s, 1)
                tap.inc(s.dma_start(out=t["tap_cc"], in_=t["cc_out"]), 16)
            if "tap_ofm" in self.taps and self.stage >= 5:
                sem["oev"].wait(s)
                tap.inc(s.dma_start(out=t["tap_ofm"], in_=t["ofm"][:]), 16)
            for tp, buf in [("tap_kbuf", "kbuf"), ("tap_vtbuf", "vtbuf"),
                            ("tap_pa", "p_a"), ("tap_pb", "p_b"), ("tap_st5", "st5"),
                            ("tap_rbs", "rb_s"), ("tap_rrow", "rrow"),
                            ("tap_lnst2", "lnst2")]:
                if tp in self.taps:
                    sem["oev"].wait(s)
                    tap.inc(s.dma_start(out=t[tp], in_=t[buf][:]), 16)
            if "tap_x2" in self.taps and self.stage >= 6:
                sem["x2d"].wait(s, 128)
                tap.inc(s.dma_start(out=t["tap_x2"], in_=t["x2h"]), 16)
            if "tap_comb" in self.taps and self.stage >= 7:
                sem["cmb"].wait(s)
                tap.inc(s.dma_start(out=t["tap_comb"], in_=t["comb"][:]), 16)
            if "tap_xhat2" in self.taps and self.stage >= 7:
                sem["xh2"].wait(s)
                tap.inc(s.dma_start(out=t["tap_xhat2"], in_=t["xhat"][:]), 16)
            if "tap_acc" in self.taps and self.stage >= 8:
                sem["dev"].wait(s)
                tap.inc(s.dma_start(out=t["tap_acc"], in_=t["acc"][:]), 16)
            if self.stage < 9:
                sem["ldx"].wait(s, 16)
                tap.inc(s.dma_start(out=t["y"].rearrange("(t p) c -> t p c", p=128)[0],
                                    in_=t["x_tm"][:, 0:H]), 16)
            sem["yst"].wait(s)
            tap.wait(s)
        self.body("sync", sync)


def rope_tables(pos):
    inv = 1.0 / (10000.0 ** (np.arange(0, ROT, 2, dtype=np.float64) / ROT))
    ang = pos[None, :].astype(np.float64) * np.concatenate([inv, inv])[:, None]
    return (np.cos(ang).astype(np.float32).astype(BF),
            np.sin(ang).astype(np.float32).astype(BF))


def prep_inputs(x, w_qkv, b_qkv, w_out, b_out, gate_w, gate_up, down):
    shared = {}
    kcols = np.concatenate([w_qkv[:, 384 * h + 128:384 * h + 256] for h in range(NH)], axis=1)
    qcols = np.concatenate([w_qkv[:, 384 * h:384 * h + 128] for h in range(NH)], axis=1)
    wqk = np.concatenate([kcols, qcols], axis=1)
    shared["wqk"] = np.ascontiguousarray(
        wqk.reshape(KC, 128, 32, 128).transpose(2, 1, 0, 3).reshape(32, 128, KC * 128))
    bk = np.concatenate([b_qkv[384 * h + 128:384 * h + 256] for h in range(NH)])
    bq = np.concatenate([b_qkv[384 * h:384 * h + 128] for h in range(NH)])
    shared["bqk"] = np.ascontiguousarray(np.concatenate([bk, bq]).reshape(32, 128).T)
    vcols = np.concatenate([w_qkv[:, 384 * h + 256:384 * h + 384] for h in range(NH)], axis=1)
    shared["wv"] = np.ascontiguousarray(
        vcols.reshape(KC, 128, 8, 256).transpose(2, 1, 0, 3).reshape(8, 128, KC * 256))
    shared["bv"] = np.ascontiguousarray(
        np.concatenate([b_qkv[384 * h + 256:384 * h + 384] for h in range(NH)])[None, :])
    shared["wout"] = np.ascontiguousarray(
        w_out.reshape(KC, 128, 16, 128).transpose(2, 1, 0, 3).reshape(16, 128, KC * 128))
    shared["bout"] = np.ascontiguousarray(b_out.reshape(16, 128).T)
    shared["gate"] = np.ascontiguousarray(
        gate_w.T.reshape(KC, 128, NE).transpose(1, 0, 2).reshape(128, KC * NE))
    ordm = [0, 6, 1, 7, 2, 8, 3, 9, 4, 10, 5, 11]
    arr = gate_up.transpose(0, 2, 1).reshape(NE, KC, 128, 12, 128)[:, :, :, ordm, :]
    arr = arr.transpose(0, 3, 2, 1, 4)
    shared["wgu"] = np.ascontiguousarray(
        arr.reshape(NE, 4, 3, 128, KC, 128).transpose(0, 1, 3, 2, 4, 5)
        .reshape(NE, 4, 128, 3 * KC * 128))
    dn = down.transpose(0, 2, 1).reshape(16, 2, 6, 128, 16, 128)
    dn = dn.transpose(0, 4, 3, 1, 2, 5)
    shared["wd"] = np.ascontiguousarray(
        dn.reshape(16, 4, 4, 128, 2, 6, 128).transpose(0, 1, 3, 2, 4, 5, 6)
        .reshape(16, 4, 128, 4 * 2 * 6 * 128))
    shared["kidx"] = np.tile(np.arange(512, dtype=np.float32)[None, :], (128, 1))
    shared["ident_bf"] = np.eye(128, dtype=BF)
    shared["ident_f"] = np.eye(128, dtype=np.float32)
    sm = np.zeros((128, ROT), dtype=BF)
    for m in range(16):
        sm[m + 16, m] = -1.0
        sm[m, m + 16] = 1.0
    shared["smat"] = sm
    selm = np.zeros((NE, NE * 128), dtype=BF)
    for e in range(NE):
        selm[e, 128 * e:128 * (e + 1)] = 1.0
    shared["selmat"] = selm
    shared["ones_bf"] = np.ones((1, 128), dtype=BF)

    in_maps = []
    for c in range(8):
        j, bb = c % 4, c // 4
        sA, sB = j, 7 - j
        rows = np.concatenate([x[bb, 512 * sA:512 * (sA + 1)],
                               x[bb, 512 * sB:512 * (sB + 1)]], axis=0)
        pos = np.concatenate([np.arange(512 * sA, 512 * (sA + 1)),
                              np.arange(512 * sB, 512 * (sB + 1))])
        cosm, sinm = rope_tables(pos)
        qi = np.arange(128, dtype=np.float32)
        dtab = np.zeros((128, 48), dtype=np.float32)
        for g, (sg, nb) in enumerate([(sA, 4), (sB, 8)]):
            for tt in range(4):
                for bi in range(nb):
                    idx = 16 * g + tt * nb + bi
                    dtab[:, idx] = 512 * (sg - bi) + 128 * tt + qi
        m = dict(shared)
        m["xin"] = np.ascontiguousarray(rows)
        m["costab"] = cosm
        m["sintab"] = sinm
        m["dtab"] = dtab
        in_maps.append(m)
    return in_maps


def kernel(L0_attn_hidden_states, L0_attn_w_qkv, L0_attn_b_qkv, L0_attn_w_out,
           L0_attn_b_out, L0_ffn_gate_weight, L0_ffn_gate_up_proj,
           L0_ffn_down_proj):
    from concourse.bass_utils import run_bass_kernel_spmd
    x = np.asarray(L0_attn_hidden_states)
    in_maps = prep_inputs(x, np.asarray(L0_attn_w_qkv), np.asarray(L0_attn_b_qkv),
                          np.asarray(L0_attn_w_out), np.asarray(L0_attn_b_out),
                          np.asarray(L0_ffn_gate_weight),
                          np.asarray(L0_ffn_gate_up_proj),
                          np.asarray(L0_ffn_down_proj))
    m = MK(taps=(), stage=9)
    res = run_bass_kernel_spmd(m.nc, in_maps, core_ids=list(range(8)))
    out = np.zeros((B, S, H), dtype=np.float32)
    for c in range(8):
        j, bb = c % 4, c // 4
        y = res.results[c]["y"]
        out[bb, 512 * j:512 * (j + 1)] = y[:512]
        out[bb, 512 * (7 - j):512 * (8 - j)] = y[512:]
    return out
